# version 02 (final)
# baseline (speedup 1.0000x reference)
import numpy as np
import concourse.bass as bass
import concourse.mybir as mybir
from concourse.bass_utils import run_bass_kernel_spmd
from concourse.aot_env import allow_ici_all_to_all
from concourse.bass_isa import ReduceOp

F32 = mybir.dt.float32
F32R = mybir.dt.float32r
BF16 = mybir.dt.bfloat16
AF = mybir.ActivationFunctionType
ALU = mybir.AluOpType
AXX = mybir.AxisListType.X

NCORES = 8
B, S, D = 2, 8192, 1024
TL = 2048
TC = 512
NCH = TL // TC
DTILES = D // 128
C1 = 1.5 * 2.0 ** 23
C2 = 1.5 * 2.0 ** 52
LN_EPS = 1e-5
ENGINES = ("sync", "vector", "scalar", "tensor", "gpsimd")

N_SUB = 1
SUBS = None
FEPS = float(np.finfo(np.float32).eps)
DBG_STAGE = None


class Prog:

    def __init__(self, nc):
        self.nc = nc
        self.q = {e: [] for e in ENGINES}
        self.cnt = {}
        self.sem_handles = {}
        self.sem_names = []

    def sem(self, name):
        if name not in self.cnt:
            self.cnt[name] = 0
            self.sem_names.append(name)
        return name

    def _append(self, eng, fn, incs=None, tickable=True):
        self.q[eng].append([fn, incs or [], tickable])

    def dma(self, qeng, out, in_, sem=None, **kw):
        s = self.sem(sem if sem else "dq_" + qeng)
        self.cnt[s] += 16
        v = self.cnt[s]
        self._append(qeng, lambda e, out=out, in_=in_, kw=kw: e.dma_start(
            out=out, in_=in_, **kw), incs=[(s, 16)], tickable=False)
        return v, s

    def tick(self, eng):
        s = self.sem("e_" + eng)
        q = self.q[eng]
        i = len(q) - 1
        while i >= 0 and not q[i][2]:
            i -= 1
        if i < 0:
            raise RuntimeError(f"tick({eng}) with no tickable instruction")
        for (s0, n0) in q[i][1]:
            if s0 == s:
                return self.cnt[s]
        self.cnt[s] += 1
        v = self.cnt[s]
        q[i][1].append((s, 1))
        return v

    def op(self, eng, meth, *args, **kw):
        import traceback
        site = traceback.extract_stack()[-2]

        def fn(e, m=meth, a=args, k=kw, site=site):
            try:
                return getattr(e, m)(*a, **k)
            except Exception as ex:
                raise RuntimeError(f"op {m} emitted at {site.filename}:{site.lineno}: {ex}") from ex
        self._append(eng, fn)
        if eng in ("vector", "scalar", "gpsimd"):
            out_ap = kw.get("out")
            if out_ap is None and args:
                out_ap = args[0]
            try:
                shp = out_ap.shape
                free = 1
                for d in shp[1:]:
                    free *= d
                small = out_ap.size() < 2048 or free < 512
            except Exception:
                try:
                    small = out_ap is not None and out_ap.size() < 2048
                except Exception:
                    small = False
            if small:
                self._append(eng, lambda e: e.drain(), tickable=False)

    def wait(self, eng, sem, val):
        if val <= 0:
            return
        self.sem(sem)
        self._append(eng, lambda e, s=sem, v=val: e.wait_ge(self.sem_handles[s], v),
                     tickable=False)

    def coll(self, kind, alu, ins, outs):
        s = self.sem("e_gpsimd")
        self.cnt[s] += 1
        v = self.cnt[s]

        def fn(e, kind=kind, alu=alu, ins=ins, outs=outs):
            with allow_ici_all_to_all():
                return e.collective_compute(kind, alu, replica_groups=[list(range(NCORES))],
                                            ins=ins, outs=outs)
        self._append("gpsimd", fn, incs=[(s, 1)])
        return v

    def finalize(self):
        nc = self.nc
        import contextlib

        def run_queue(em, entries):
            for fn, incs, _ in entries:
                inst = fn(em)
                for (s, n) in incs:
                    inst = inst.then_inc(self.sem_handles[s], n)

        with contextlib.ExitStack() as stack:
            for name in self.sem_names:
                self.sem_handles[name] = stack.enter_context(nc.semaphore(name))
            block = stack.enter_context(nc.Block())
            for eng in ENGINES:
                entries = self.q[eng]
                getattr(block, eng)(lambda em, entries=entries: run_queue(em, entries))


def build_program(n_sub=N_SUB, DBG_STAGE=None, ffn_phases=5, ffn_chunks=NCH, subs=None, moe_erange=(0, 17)):
    nc = bass.Bass("TRN2", target_bir_lowering=False, debug=False, detect_race_conditions=False)
    p = Prog(nc)

    def din(name, shape, dtype=F32):
        return nc.dram_tensor(name, list(shape), dtype, kind="ExternalInput").ap()

    def sb(name, shape, dtype=F32):
        return nc.alloc_sbuf_tensor(name, list(shape), dtype).ap()

    x_in = din("x_in", (D, TL))
    x_out = nc.dram_tensor("x_out", [D, TL], F32, kind="ExternalOutput").ap()
    w1T_d = din("L0_w1T", (D, 4096))
    b1_d = din("L0_b1c", (128, 32))
    w2T_d = din("L0_w2T", (4096, D))
    b2_d = din("L0_b2c", (128, 8))
    seed_d = din("seed", (1, 1))
    ident_d = din("ident", (128, 128))
    gate_wT_d = din("L1_gate_wT", (D, 16))
    ebias_d = din("L1_ebias", (1, 16))
    gpT_d = din("L1_gpT", (16, D, 512))
    upT_d = din("L1_upT", (16, D, 512))
    dnT_d = din("L1_dnT", (16, 512, D))
    sgT_d = din("L1_sgT", (D, 512))
    suT_d = din("L1_suT", (D, 512))
    sdT_d = din("L1_sdT", (512, D))
    iota_d = din("iota_desc", (128, 16))
    onesb_d = din("ones_b", (1, 128), BF16)

    dbg_out = nc.dram_tensor("dbg_out", [32, 128, TC], F32, kind="ExternalOutput").ap()
    w1q_s = nc.dram_tensor("w1q_s", [32, 128, 8, 128], BF16).ap()
    acols_d = din("acols_in", (128, 112))
    aWin_d = din("aW_in", (1024, 4096))
    aWout_d = din("aW_out", (2048, 1024))
    bWin_d = din("bW_inT", (1024, 4096))
    bWout_d = din("bW_outT", (2048, 1024))
    axpT_d = din("a_xpT", (2, 128, 33))
    bxpT_d = din("b_xpT", (2, 128, 96))
    bdtwT_d = din("b_dtwT", (64, 256))
    minT_d = din("m_inT", (64, 128))
    mxpT_d = din("m_xpT", (64, 12))
    mdtwT_d = din("m_dtwT", (4, 64))
    moutT_d = din("m_outT", (64, 64))
    snd_x = nc.dram_tensor("snd_x", [8, 2, 128, 2048], F32).ap()
    rcv_x = nc.dram_tensor("rcv_x", [8, 2, 128, 2048], F32).ap()
    snd_y = nc.dram_tensor("snd_y", [8, 2, 128, 2048], F32).ap()
    rcv_y = nc.dram_tensor("rcv_y", [8, 2, 128, 2048], F32).ap()
    z_sp = nc.dram_tensor("z_sp", [16, 128, 2048], F32).ap()
    u_d = nc.dram_tensor("u_d", [2, 2, 128, 8192], F32).ap()
    ar0_i = nc.dram_tensor("ar0_i", [2, 33, 8192], F32).ap()
    ar0_o = nc.dram_tensor("ar0_o", [2, 33, 8192], F32).ap()
    ar1_i = nc.dram_tensor("ar1_i", [2, 96, 8192], F32).ap()
    ar1_o = nc.dram_tensor("ar1_o", [2, 96, 8192], F32).ap()
    xzm_d = nc.dram_tensor("xzm_d", [2, 128, 8192], F32).ap()
    xdm_d = nc.dram_tensor("xdm_d", [2, 12, 8192], F32).ap()
    dtm_d = nc.dram_tensor("dtm_d", [2, 64, 8192], F32).ap()
    W_dram = nc.dram_tensor("W_dram", [16, TL], BF16).ap()
    w2q_s = nc.dram_tensor("w2q_s", [8, 128, 32, 128], BF16).ap()

    x_t = [sb(f"x{i}", (128, TL)) for i in range(DTILES)]
    wst1 = [sb(f"wst1_{i}", (128, 1024), BF16) for i in range(2)]
    wst2 = [sb("wst2_0", (128, 4096), BF16)] * 2
    c_b = sb("c_b", (128, TC))
    inv_b = sb("inv_b", (128, TC))
    sq_s = [sb("sq0", (128, TC), F32R)] * 2
    p_s = [sb(f"p{i}", (128, TC)) for i in range(2)]
    amax_s = sb("amax_s", (128, TC))
    ones_rr = sb("ones_rr", (1, 128), F32R)
    rr_a = sb("rr_a", (1, TC), F32R)
    rr_b = sb("rr_b", (1, TC), F32R)
    rowS_r = sb("rowS_r", (1, 16), F32R)
    colA_r = sb("colA_r", (128, 2), F32R)
    ones_cr = sb("ones_cr", (128, 1), F32R)
    ind_c = sb("ind_c", (128, 1))
    b1c = sb("b1c", (128, 32))
    b2h = sb("b2h", (128, 8))
    colA = sb("colA", (128, 1))
    colB = sb("colB", (128, 1))
    sw_cols = {4: sb("swc4", (128, 1)), 6: sb("swc6", (128, 1))}
    rowS = sb("rowS", (1, 16))
    ident = sb("ident_sb", (128, 128))
    m4 = sb("m4", (128, 4))
    r_mean = sb("r_mean", (1, TC))
    r_var = sb("r_var", (1, TC))
    r_rstd = sb("r_rstd", (1, TC))
    r_t0 = sb("r_t0", (1, TC))

    xr_s = [sb("xr0", (128, TC), F32R)] * 2
    acols = sb("acols", (128, 112))
    gateW = sb("gateW", (128, 128), BF16)
    ebias_b = sb("ebias_b", (128, 16))
    iota_b = sb("iota_b", (128, 16))
    ones_rb = sb("ones_rb", (1, 128), BF16)
    wrow_b = sb("wrow_b", (1, 256), BF16)
    wrep_s = sb("wrep_s", (128, 256))
    inter_b = sb("inter_b", (128, 1024), BF16)
    wtmp_b = inter_b[0:16, 0:128]
    psb = [nc.alloc_psum_tensor(f"pb{i}", [128, 512], F32).ap() for i in range(8)]
    ROW_A, ROW_B = psb[6][0:1, :], psb[7][0:1, :]

    for s_ in ("e_vector", "e_scalar", "e_tensor", "e_gpsimd", "dq_sync", "dq_scalar", "dq_gpsimd"):
        p.sem(s_)

    bank_free = [0] * 8

    def bank_wait(*banks):
        for b_ in banks:
            if bank_free[b_]:
                if isinstance(bank_free[b_], tuple):
                    p.wait("tensor", bank_free[b_][0], bank_free[b_][1])
                else:
                    p.wait("tensor", "e_vector", bank_free[b_])
                bank_free[b_] = 0

    def pmax_bcast(src):
        vts = []
        for t in range(TC // 128):
            bk = 5 if t % 2 == 0 else 7
            bank_wait(bk)
            p.op("tensor", "transpose", psb[bk][:, 0:128], src[:, 128 * t:128 * (t + 1)], ident)
            tt = p.tick("tensor")
            p.wait("vector", "e_tensor", tt)
            p.op("vector", "reduce_max", out=m4[:, t:t + 1], in_=psb[bk][:, 0:128], axis=AXX)
            vt = p.tick("vector")
            bank_free[bk] = vt
            vts.append(vt)
        bank_wait(0, 1, 2, 3)
        p.wait("tensor", "e_vector", vts[-1])
        for t in range(TC // 128):
            p.op("tensor", "transpose", psb[t][0:1, 0:128], m4[:, t:t + 1], ident)
        tt2 = p.tick("tensor")
        p.wait("vector", "e_tensor", tt2)
        for t in range(TC // 128):
            p.op("vector", "tensor_copy", out=rr_a[:, 128 * t:128 * (t + 1)], in_=psb[t][0:1, 0:128])
        vr = p.tick("vector")
        for t in range(TC // 128):
            bank_free[t] = vr
        bank_wait(4)
        p.wait("tensor", "e_vector", vr)
        p.op("tensor", "matmul", psb[4], ones_rr, rr_a, start=True, stop=True)
        return p.tick("tensor")

    def q_round(src, dst, crep_ps, w=TC):
        p.op("vector", "tensor_mul", out=p_s[0][:, 0:w], in0=src, in1=crep_ps)
        p.op("vector", "tensor_scalar", out=p_s[1][:, 0:w], in0=p_s[0][:, 0:w],
             scalar1=C1, scalar2=C1, op0=ALU.add, op1=ALU.subtract)
        p.op("vector", "tensor_scalar", out=p_s[0][:, 0:w], in0=p_s[1][:, 0:w],
             scalar1=C2, scalar2=C2, op0=ALU.add, op1=ALU.subtract)
        p.op("vector", "tensor_sub", out=p_s[0][:, 0:w], in0=p_s[0][:, 0:w], in1=p_s[1][:, 0:w])
        p.op("vector", "scalar_tensor_tensor", out=dst, in0=p_s[0][:, 0:w],
             scalar=ind_c, in1=p_s[1][:, 0:w], op0=ALU.mult, op1=ALU.add)

    for i in range(DTILES):
        p.dma("sync", x_t[i], x_in[128 * i:128 * (i + 1), :], sem="dq_boot")
    p.dma("sync", rowS[:, 0:1], seed_d, sem="dq_boot")
    p.dma("sync", b1c, b1_d, sem="dq_boot")
    p.dma("sync", b2h, b2_d, sem="dq_boot")
    p.dma("sync", ident, ident_d, sem="dq_boot")
    BOOT = p.cnt["dq_boot"]
    p.op("gpsimd", "memset", ones_cr.bitcast(F32), 1.0)
    p.op("gpsimd", "memset", ones_rr.bitcast(F32), 1.0)

    g0 = p.tick("gpsimd")

    p.wait("tensor", "dq_boot", BOOT)
    p.wait("vector", "dq_boot", BOOT)
    p.op("vector", "tensor_scalar", out=rowS[:, 1:2], in0=rowS[:, 0:1], scalar1=1.0, scalar2=None, op0=ALU.add)
    p.op("vector", "tensor_scalar", out=rowS[:, 2:3], in0=rowS[:, 1:2], scalar1=1.0, scalar2=None, op0=ALU.subtract)
    p.op("vector", "tensor_scalar", out=rowS[:, 3:4], in0=rowS[:, 2:3], scalar1=2.0 ** 40, scalar2=None, op0=ALU.mult)
    vind = p.tick("vector")
    p.wait("scalar", "e_vector", vind)
    p.op("scalar", "copy", out=rowS_r[:, 0:1], in_=rowS[:, 3:4])
    p.op("scalar", "copy", out=rowS_r[:, 1:2], in_=rowS[:, 3:4])
    sind = p.tick("scalar")
    p.wait("tensor", "e_scalar", sind)
    p.wait("tensor", "e_gpsimd", g0)
    p.op("tensor", "matmul", psb[4][:, 0:2], ones_rr, rowS_r[:, 0:2], start=True, stop=True)
    tind = p.tick("tensor")
    p.wait("vector", "e_tensor", tind)
    p.op("vector", "tensor_copy", out=ind_c, in_=psb[4][:, 0:1])
    v_ind = p.tick("vector")
    bank_free[4] = v_ind
    p.wait("scalar", "dq_boot", BOOT)
    p.op("scalar", "mul", out=b2h, in_=b2h, mul=0.5)
    p.tick("scalar")

    mld, mlsem = p.dma("gpsimd", gateW.rearrange("p (k e) -> p k e", e=16),
                       gate_wT_d.rearrange("(k p) e -> p k e", p=128), sem="dq_moe0")
    mld, mlsem = p.dma("sync", iota_b, iota_d, sem="dq_boot2")
    mld, mlsem = p.dma("sync", ones_rb, onesb_d, sem="dq_boot2")
    p.wait("scalar", "dq_boot2", p.cnt["dq_boot2"])
    p.wait("scalar", "dq_boot", BOOT)
    eb_r = sb("eb_r", (1, 16), F32R)
    eld, elsem = p.dma("gpsimd", eb_r, ebias_d, sem="dq_moe0")
    p.wait("tensor", "dq_boot2", p.cnt["dq_boot2"])
    p.wait("tensor", "dq_moe0", eld)
    bank_wait(5)
    p.op("tensor", "matmul", psb[5][:, 0:16], ones_rr, eb_r, start=True, stop=True)
    t_eb = p.tick("tensor")
    p.wait("vector", "e_tensor", t_eb)
    p.op("vector", "tensor_copy", out=ebias_b, in_=psb[5][:, 0:16])
    bank_free[5] = p.tick("vector")

    acl, acls = p.dma("sync", acols, acols_d, sem="dq_boot2")
    p.wait("vector", "dq_boot2", acl)
    for c0, c1 in ((0, 4), (48, 50), (99, 100)):
        p.op("vector", "tensor_scalar_mul", out=acols[:, c0:c1], in0=acols[:, c0:c1],
             scalar1=-1.0)
    v_ac = p.tick("vector")
    p.wait("scalar", "dq_boot2", acl)
    for c0, c1 in ((4, 36), (50, 82), (94, 98)):
        p.op("scalar", "activation", out=acols[:, c0:c1], in_=acols[:, c0:c1], func=AF.Exp)
    s_ac = p.tick("scalar")

    wb_g = [nc.sbuf_tensor(f"wbig{i}", [128, 4096], F32) for i in range(2)]
    wq_g = nc.sbuf_tensor("wq_b", [128, 4096], BF16)
    wbig = [g.__enter__().ap() for g in wb_g]
    wq_b = wq_g.__enter__().ap()

    def emit_wquant(wT, KD, FD, scratch, si, tag, prev_store=None):
        nblk = KD // 128
        V0 = p.cnt["e_vector"]
        p.op("gpsimd", "memset", colA, 0.0)
        gg = p.tick("gpsimd")
        p.wait("vector", "e_gpsimd", gg)
        vticks = []
        for k in range(nblk):
            if k >= 2:
                p.wait("scalar", "e_vector", vticks[k - 2])
            else:
                p.wait("scalar", "e_vector", V0)
            ld, lsem = p.dma("scalar", wbig[k % 2][:, 0:FD], wT[128 * k:128 * (k + 1), :],
                             sem=f"dq_{tag}a{k % 2}")
            p.wait("vector", lsem, ld)
            p.op("vector", "tensor_reduce", out=colB, in_=wbig[k % 2][:, 0:FD], axis=AXX,
                 op=ALU.add, apply_absolute_value=True)
            p.op("vector", "tensor_add", out=colA, in0=colA, in1=colB)
            vticks.append(p.tick("vector"))
        vpre = p.tick("vector")
        bank_wait(6)
        p.wait("scalar", "e_vector", vpre)
        p.op("scalar", "copy", out=colA_r[:, 0:1], in_=colA)
        p.op("scalar", "copy", out=colA_r[:, 1:2], in_=colA)
        spre = p.tick("scalar")
        p.wait("tensor", "e_scalar", spre)
        p.op("tensor", "matmul", psb[6][0:1, 0:2], ones_cr, colA_r, start=True, stop=True)
        tred = p.tick("tensor")
        p.wait("vector", "e_tensor", tred)
        p.op("vector", "tensor_scalar", out=rowS[:, si:si + 1], in0=psb[6][0:1, 0:1],
             scalar1=1.0 / (KD * FD), scalar2=1e-5, op0=ALU.mult, op1=ALU.max)
        vms = p.tick("vector")
        bank_free[6] = vms
        bank_wait(4)
        p.wait("scalar", "e_vector", vms)
        p.op("scalar", "copy", out=rowS_r[:, si:si + 1], in_=rowS[:, si:si + 1])
        p.op("scalar", "copy", out=rowS_r[:, si + 1:si + 2], in_=rowS[:, si:si + 1])
        sms = p.tick("scalar")
        p.wait("tensor", "e_scalar", sms)
        p.op("tensor", "matmul", psb[4][:, 0:2], ones_rr, rowS_r[:, si:si + 2], start=True, stop=True)
        tbc = p.tick("tensor")
        p.wait("vector", "e_tensor", tbc)
        p.op("vector", "tensor_copy", out=sw_cols[si], in_=psb[4][:, 0:1])
        p.op("vector", "reciprocal", out=colB, in_=sw_cols[si])
        vscale = p.tick("vector")
        bank_free[4] = vscale

        last_st = 0
        vticks = []
        gticks = []
        for k in range(nblk):
            if k >= 2:
                p.wait("scalar", "e_vector", vticks[k - 2])
            else:
                p.wait("scalar", "e_vector", vscale)
            ld, lsem = p.dma("scalar", wbig[k % 2][:, 0:FD], wT[128 * k:128 * (k + 1), :],
                             sem=f"dq_{tag}b{k % 2}")
            p.wait("vector", lsem, ld)
            if k == 0:
                p.wait("vector", "e_vector", v_ind)
                if prev_store is not None:
                    p.wait("vector", prev_store[1], prev_store[0])
            if k >= 1:
                p.wait("vector", gticks[k - 1][1], gticks[k - 1][0])
            for h0 in range(0, FD, TC):
                hs = slice(h0, h0 + TC)
                p.op("vector", "tensor_scalar_mul", out=p_s[0][:, 0:TC], in0=wbig[k % 2][:, hs], scalar1=colB)
                p.op("vector", "tensor_scalar", out=p_s[1][:, 0:TC], in0=p_s[0][:, 0:TC],
                     scalar1=C1, scalar2=C1, op0=ALU.add, op1=ALU.subtract)
                p.op("vector", "tensor_scalar", out=p_s[0][:, 0:TC], in0=p_s[1][:, 0:TC],
                     scalar1=C2, scalar2=C2, op0=ALU.add, op1=ALU.subtract)
                p.op("vector", "tensor_sub", out=p_s[0][:, 0:TC], in0=p_s[0][:, 0:TC], in1=p_s[1][:, 0:TC])
                p.op("vector", "scalar_tensor_tensor", out=p_s[0][:, 0:TC], in0=p_s[0][:, 0:TC],
                     scalar=ind_c, in1=p_s[1][:, 0:TC], op0=ALU.mult, op1=ALU.add)
                p.op("vector", "tensor_scalar", out=wq_b[:, hs], in0=p_s[0][:, 0:TC],
                     scalar1=1.0, scalar2=-1.0, op0=ALU.min, op1=ALU.max)
            vt = p.tick("vector")
            vticks.append(vt)
            p.wait("gpsimd", "e_vector", vt)
            last_st = p.dma("gpsimd", scratch[:, :, k, :].transpose([1, 0, 2]),
                            wq_b[:, 0:FD].rearrange("p (a c) -> p a c", c=128),
                            sem=f"dq_{tag}s")
            gticks.append(last_st)
        return last_st

    wq1_done, wq1_sem = emit_wquant(w1T_d, D, 4096, w1q_s, 4, "q1w")
    wq2_done, wq2_sem = emit_wquant(w2T_d, 4096, D, w2q_s, 6, "q2w",
                                    prev_store=(wq1_done, wq1_sem))
    wq_g.__exit__(None, None, None)
    for g in reversed(wb_g):
        g.__exit__(None, None, None)

    arenaF = sb("arenaF", (128, 32 * TC))
    arenaB = sb("arenaB", (128, 32 * TC), BF16)
    h_t = [arenaF[:, TC * i:TC * (i + 1)] for i in range(32)]
    q2_t = [arenaB[:, TC * i:TC * (i + 1)] for i in range(32)]
    q1_t = q2_t[:DTILES]
    xn_t = h_t[:DTILES]

    def stat_ssq(tiles, cs, row_ps, ntiles, with_max=False):
        tticks = {}
        vticks = {}
        for i in range(ntiles):
            if i >= 1:
                p.wait("scalar", "e_tensor", tticks[i - 1])
                if with_max:
                    p.wait("scalar", "e_vector", vticks[i - 1])
            p.op("scalar", "activation", out=sq_s[i % 2],
                 in_=(tiles[i] if cs is None else tiles[i][:, cs]), func=AF.Square)
            st_ = p.tick("scalar")
            p.wait("tensor", "e_scalar", st_)
            p.op("tensor", "matmul", row_ps, ones_cr, sq_s[i % 2][:, :],
                 start=(i == 0), stop=(i == ntiles - 1))
            tticks[i] = p.tick("tensor")
            if with_max:
                p.wait("vector", "e_scalar", st_)
                if i == 0:
                    p.op("vector", "tensor_copy", out=amax_s, in_=sq_s[0].bitcast(F32))
                else:
                    p.op("vector", "tensor_max", out=amax_s, in0=amax_s,
                         in1=sq_s[i % 2].bitcast(F32))
                vticks[i] = p.tick("vector")
        if with_max:
            return tticks[ntiles - 1], vticks[ntiles - 1]
        return tticks[ntiles - 1]

    def ln_chunk(cs, prev_v, upto=4, out_tiles=None):
        if not hasattr(ln_chunk, "first"):
            ln_chunk.first = True
        if out_tiles is None:
            out_tiles = xn_t
        """layernorm chunk of x -> xn_t (h_t[0..7]); prev_v gates xn overwrite."""
        p.wait("scalar", "e_vector", prev_v)
        p.wait("tensor", "e_vector", prev_v)
        bank_wait(6, 7)
        tticks = {}
        for i in range(DTILES):
            if i >= 1:
                p.wait("scalar", "e_tensor", tticks[i - 1])
            p.op("scalar", "copy", out=xr_s[i % 2], in_=x_t[i][:, cs])
            p.op("scalar", "activation", out=sq_s[i % 2], in_=x_t[i][:, cs], func=AF.Square)
            st_ = p.tick("scalar")
            p.wait("tensor", "e_scalar", st_)
            p.op("tensor", "matmul", ROW_A, ones_cr, xr_s[i % 2][:, :],
                 start=(i == 0), stop=(i == DTILES - 1))
            p.op("tensor", "matmul", ROW_B, ones_cr, sq_s[i % 2][:, :],
                 start=(i == 0), stop=(i == DTILES - 1))
            tticks[i] = p.tick("tensor")
        t_red = tticks[DTILES - 1]
        if upto <= 1:
            return p.tick("vector")
        p.wait("vector", "e_tensor", t_red)
        p.op("vector", "tensor_scalar_mul", out=r_mean, in0=ROW_A, scalar1=1.0 / D)
        p.op("vector", "tensor_scalar_mul", out=r_var, in0=ROW_B, scalar1=1.0 / D)
        p.op("vector", "tensor_mul", out=r_t0, in0=r_mean, in1=r_mean)
        p.op("vector", "tensor_sub", out=r_var, in0=r_var, in1=r_t0)
        p.op("vector", "tensor_scalar_add", out=r_var, in0=r_var, scalar1=LN_EPS)
        vv = p.tick("vector")
        bank_free[6] = bank_free[7] = vv
        if DBG_STAGE == "lnrows" and ln_chunk.first:
            ln_chunk.first = False
            p.wait("gpsimd", "e_vector", vv)
            p.dma("gpsimd", dbg_out[0][0:1, :], r_mean, sem="dq_dbg")
            p.dma("gpsimd", dbg_out[1][0:1, :], r_var, sem="dq_dbg")
            p.wait("vector", "dq_dbg", p.cnt["dq_dbg"])
            p.wait("scalar", "dq_dbg", p.cnt["dq_dbg"])
        p.wait("scalar", "e_vector", vv)
        p.op("scalar", "activation", out=r_var, in_=r_var, func=AF.Sqrt)
        sv = p.tick("scalar")
        p.wait("vector", "e_scalar", sv)
        p.op("vector", "reciprocal", out=r_rstd, in_=r_var)
        p.op("vector", "tensor_mul", out=r_t0, in0=r_mean, in1=r_rstd)
        vv2 = p.tick("vector")
        if upto <= 2:
            return vv2
        p.wait("scalar", "e_vector", vv2)
        p.op("scalar", "copy", out=rr_a, in_=r_rstd)
        p.op("scalar", "copy", out=rr_b, in_=r_t0)
        sv2 = p.tick("scalar")
        p.wait("tensor", "e_scalar", sv2)
        p.wait("tensor", "e_vector", max(vv2, prev_v))
        bank_wait(4, 5)
        p.op("tensor", "matmul", psb[4], ones_rr, rr_a, start=True, stop=True)
        p.op("tensor", "matmul", psb[5], ones_rr, rr_b, start=True, stop=True)
        tb = p.tick("tensor")
        if upto <= 3:
            p.wait("vector", "e_tensor", tb)
            v3 = p.tick("vector")
            bank_free[4] = bank_free[5] = v3
            return v3
        p.wait("vector", "e_tensor", tb)
        for i in range(DTILES):
            p.op("vector", "tensor_mul", out=p_s[0], in0=x_t[i][:, cs], in1=psb[4])
            p.op("vector", "tensor_sub", out=out_tiles[i], in0=p_s[0], in1=psb[5])
        v_last = p.tick("vector")
        bank_free[4] = bank_free[5] = v_last
        return v_last

    guards = {}

    def emit_ffn_sublayer(si1, si2, prev_v0=0):
        w1_guard = {}
        w2_guard = {}
        if guards.get("wst2"):
            w2_guard[0] = guards["wst2"]
        if guards.get("wst1"):
            w1_guard[0] = w1_guard[1] = guards["wst1"]
        prev_v = prev_v0
        for c in range(ffn_chunks):
            cs = slice(TC * c, TC * (c + 1))
            ln_v = ln_chunk(cs, prev_v, upto=(4 if ffn_phases >= 1.99 else int(ffn_phases * 4 + 0.5)))
            if ffn_phases <= 1:
                prev_v = ln_v
                continue
            if c == 0 and DBG_STAGE == "ln":
                p.wait("gpsimd", "e_vector", ln_v)
                for i in range(DTILES):
                    p.dma("gpsimd", dbg_out[i], xn_t[i], sem="dq_dbg")
                p.wait("vector", "dq_dbg", p.cnt["dq_dbg"])

            bank_wait(6)
            p.wait("scalar", "e_vector", ln_v)
            t_red, v_amax = stat_ssq(xn_t, None, ROW_A, DTILES, with_max=True)
            p.wait("tensor", "e_vector", v_amax)
            if c == 0 and DBG_STAGE == "cb":
                vpre = p.tick("vector")
                p.wait("gpsimd", "e_vector", vpre)
                p.dma("gpsimd", dbg_out[3], amax_s, sem="dq_dbg")
            t_am = pmax_bcast(amax_s)
            if c == 0 and DBG_STAGE == "cb":
                p.wait("gpsimd", "e_tensor", t_am)
                p.dma("gpsimd", dbg_out[2][0:1, :], r_t0[:, 0:TC], sem="dq_dbg")
            p.wait("scalar", "e_tensor", t_am)
            p.op("scalar", "activation", out=p_s[1], in_=psb[4], func=AF.Sqrt)
            s_am = p.tick("scalar")
            bank_free[4] = ("e_scalar", s_am)
            p.wait("vector", "e_tensor", t_red)
            p.op("vector", "tensor_scalar", out=r_var, in0=ROW_A,
                 scalar1=1.0 / D, scalar2=LN_EPS, op0=ALU.mult, op1=ALU.add)
            vv = p.tick("vector")
            bank_free[6] = vv
            p.wait("scalar", "e_vector", vv)
            p.op("scalar", "activation", out=r_var, in_=r_var, func=AF.Sqrt)
            sv = p.tick("scalar")
            p.wait("vector", "e_scalar", sv)
            p.op("vector", "reciprocal", out=r_rstd, in_=r_var)
            v_rs = p.tick("vector")
            p.wait("scalar", "e_vector", v_rs)
            p.op("scalar", "copy", out=rr_a, in_=r_rstd)
            s_rs = p.tick("scalar")
            bank_wait(4)
            p.wait("tensor", "e_scalar", s_rs)
            p.op("tensor", "matmul", psb[4], ones_rr, rr_a, start=True, stop=True)
            tb = p.tick("tensor")
            p.wait("vector", "e_tensor", tb)
            p.op("vector", "tensor_mul", out=c_b, in0=p_s[1], in1=psb[4])
            p.op("vector", "tensor_scalar_max", out=c_b, in0=c_b, scalar1=1e-5)
            p.op("vector", "tensor_scalar", out=inv_b, in0=c_b, scalar1=1.0 / 127.0,
                 scalar2=sw_cols[si1], op0=ALU.mult, op1=ALU.mult)
            p.op("vector", "reciprocal", out=c_b, in_=c_b)
            p.op("vector", "tensor_scalar_mul", out=c_b, in0=c_b, scalar1=127.0)
            p.op("vector", "tensor_mul", out=c_b, in0=c_b, in1=psb[4])
            if c == 0 and DBG_STAGE == "cb":
                vcb = p.tick("vector")
                p.wait("gpsimd", "e_vector", vcb)
                p.dma("gpsimd", dbg_out[0], c_b, sem="dq_dbg")
                p.dma("gpsimd", dbg_out[1], p_s[1], sem="dq_dbg")
                p.dma("gpsimd", dbg_out[2][:, 0:1], ind_c, sem="dq_dbg", allow_slow_non_contiguous=True)
                p.dma("gpsimd", dbg_out[3][0:1, 0:16], rowS, sem="dq_dbg")
                p.dma("gpsimd", dbg_out[4][0:1, 0:16], rowS_r.bitcast(F32), sem="dq_dbg")
            for i in range(DTILES):
                q_round(xn_t[i], q1_t[i], c_b)
            vq1 = p.tick("vector")
            bank_free[4] = vq1
            if ffn_phases <= 2:
                prev_v = vq1
                continue
            if c == 0 and DBG_STAGE == "q1":
                p.wait("gpsimd", "e_vector", vq1)
                for i in range(DTILES):
                    p.dma("gpsimd", dbg_out[i], q1_t[i], sem="dq_dbg")

            tmms = {}
            last_ep = 0
            for po in range(32):
                if c == 0 and po == 0:
                    p.wait("sync", "dq_q1ws", wq1_done)
                if po >= 2:
                    p.wait("sync", "e_tensor", tmms[po - 2])
                elif w1_guard.get(po % 2):
                    p.wait("sync", "e_tensor", w1_guard[po % 2])
                pw, psem = p.dma("sync", wst1[po % 2], w1q_s[po].rearrange("p k c -> p (k c)"),
                                 sem=f"dq_st1{po % 2}")
                p.wait("tensor", psem, pw)
                p.wait("tensor", "e_vector", vq1)
                if c == 0 and po == 0 and DBG_STAGE == "kraw":
                    p.wait("gpsimd", psem, pw)
                    p.dma("gpsimd", dbg_out[30][:, 0:TC], wst1[0][:, 0:512], sem="dq_dbg")
                    p.dma("gpsimd", dbg_out[31][:, 0:TC], wst1[0][:, 512:1024], sem="dq_dbg")
                bi = po % 4
                bank_wait(bi)
                for k in range(DTILES):
                    p.op("tensor", "matmul", psb[bi], wst1[po % 2][:, 128 * k:128 * (k + 1)],
                         q1_t[k], start=(k == 0), stop=(k == DTILES - 1))
                tmms[po] = p.tick("tensor")
                w1_guard[po % 2] = tmms[po]
                p.wait("vector", "e_tensor", tmms[po])
                if c == 0 and po == 0 and DBG_STAGE == "kraw":
                    p.op("vector", "tensor_copy", out=p_s[0], in_=psb[bi])
                    vkr = p.tick("vector")
                    p.wait("gpsimd", "e_vector", vkr)
                    p.dma("gpsimd", dbg_out[0], p_s[0], sem="dq_dbg")
                p.op("vector", "tensor_mul", out=h_t[po], in0=psb[bi], in1=inv_b)
                vep = p.tick("vector")
                if c == 0 and DBG_STAGE == "hpre" and po < 31:
                    p.wait("gpsimd", "e_vector", vep)
                    p.dma("gpsimd", dbg_out[po], h_t[po], sem="dq_dbg")
                    if po == 0:
                        p.dma("gpsimd", dbg_out[31], inv_b, sem="dq_dbg")
                bank_free[bi] = vep
                p.wait("scalar", "e_vector", vep)
                p.op("scalar", "activation", out=h_t[po], in_=h_t[po], func=AF.Gelu,
                     bias=b1c[:, po:po + 1])
                last_ep = p.tick("scalar")

            if c == 0 and DBG_STAGE == "h":
                p.wait("gpsimd", "e_scalar", last_ep)
                for i in range(32):
                    p.dma("gpsimd", dbg_out[i], h_t[i], sem="dq_dbg")
                p.wait("vector", "dq_dbg", p.cnt["dq_dbg"])
                p.wait("scalar", "dq_dbg", p.cnt["dq_dbg"])
            if ffn_phases <= 3:
                prev_v = p.cnt["e_vector"]
                continue
            bank_wait(6)
            p.wait("tensor", "e_scalar", last_ep)
            t_red2, v_amax2 = stat_ssq(h_t, None, ROW_A, 32, with_max=True)
            p.wait("tensor", "e_vector", v_amax2)
            t_am2 = pmax_bcast(amax_s)
            p.wait("scalar", "e_tensor", t_am2)
            p.op("scalar", "activation", out=p_s[1], in_=psb[4], func=AF.Sqrt)
            s_am2 = p.tick("scalar")
            bank_free[4] = ("e_scalar", s_am2)
            p.wait("vector", "e_tensor", t_red2)
            p.op("vector", "tensor_scalar", out=r_var, in0=ROW_A,
                 scalar1=1.0 / 4096, scalar2=LN_EPS, op0=ALU.mult, op1=ALU.add)
            vv3 = p.tick("vector")
            bank_free[6] = vv3
            p.wait("scalar", "e_vector", vv3)
            p.op("scalar", "activation", out=r_var, in_=r_var, func=AF.Sqrt)
            sv3 = p.tick("scalar")
            p.wait("vector", "e_scalar", sv3)
            p.op("vector", "reciprocal", out=r_rstd, in_=r_var)
            v_rs2 = p.tick("vector")
            p.wait("scalar", "e_vector", v_rs2)
            p.op("scalar", "copy", out=rr_a, in_=r_rstd)
            s_rs2 = p.tick("scalar")
            bank_wait(4)
            p.wait("tensor", "e_scalar", s_rs2)
            p.op("tensor", "matmul", psb[4], ones_rr, rr_a, start=True, stop=True)
            tb2 = p.tick("tensor")
            p.wait("vector", "e_tensor", tb2)
            p.op("vector", "tensor_mul", out=c_b, in0=p_s[1], in1=psb[4])
            p.op("vector", "tensor_scalar_max", out=c_b, in0=c_b, scalar1=1e-5)
            p.op("vector", "tensor_scalar", out=inv_b, in0=c_b, scalar1=0.5 / 127.0,
                 scalar2=sw_cols[si2], op0=ALU.mult, op1=ALU.mult)
            p.op("vector", "reciprocal", out=c_b, in_=c_b)
            p.op("vector", "tensor_scalar_mul", out=c_b, in0=c_b, scalar1=127.0)
            p.op("vector", "tensor_mul", out=c_b, in0=c_b, in1=psb[4])
            for i in range(32):
                q_round(h_t[i], q2_t[i], c_b)
            vq2 = p.tick("vector")
            bank_free[4] = vq2
            if ffn_phases <= 4:
                prev_v = vq2
                continue
            if c == 0 and DBG_STAGE == "q2":
                p.wait("gpsimd", "e_vector", vq2)
                for i in range(32):
                    p.dma("gpsimd", dbg_out[i], q2_t[i], sem="dq_dbg")

            tmm2s = {}
            for po in range(8):
                if c == 0 and po == 0:
                    p.wait("sync", "dq_q2ws", wq2_done)
                if po >= 1:
                    p.wait("sync", "e_tensor", tmm2s[po - 1])
                elif w2_guard.get(0):
                    p.wait("sync", "e_tensor", w2_guard[0])
                pw, psem = p.dma("sync", wst2[po % 2], w2q_s[po].rearrange("p k c -> p (k c)"),
                                 sem="dq_st2")
                p.wait("tensor", psem, pw)
                p.wait("tensor", "e_vector", vq2)
                bi = po % 4
                bank_wait(bi)
                for k in range(32):
                    p.op("tensor", "matmul", psb[bi], wst2[po % 2][:, 128 * k:128 * (k + 1)],
                         q2_t[k], start=(k == 0), stop=(k == 31))
                tmm2s[po] = p.tick("tensor")
                w2_guard[0] = guards["wst2"] = tmm2s[po]
                p.wait("vector", "e_tensor", tmm2s[po])
                p.op("vector", "tensor_mul", out=p_s[0][:, 0:TC], in0=psb[bi], in1=inv_b)
                p.op("vector", "scalar_tensor_tensor", out=x_t[po][:, cs], in0=p_s[0][:, 0:TC],
                     scalar=b2h[:, po:po + 1], in1=x_t[po][:, cs], op0=ALU.add, op1=ALU.add)
                vmm2 = p.tick("vector")
                bank_free[bi] = vmm2
            prev_v = p.cnt["e_vector"]
        return prev_v


    TCM = 256
    xnb_t = [arenaB[:, 1024 * i:1024 * (i + 1)] for i in range(DTILES)]
    wg_v = wst2[0]
    wu_v = arenaB[:, 8192:12288]
    wd_v = arenaB[:, 12288:16384]
    sc_em = arenaF[0:16, 0:TL]
    SEL = TL + 64
    TCM = 256

    def emit_moe_sublayer(prev_v0):
        prev_v = prev_v0
        for half in range(2):
            for cl in range(2):
                c = 2 * half + cl
                cs = slice(TC * c, TC * (c + 1))
                xnb_out = [xnb_t[i][:, TC * cl:TC * (cl + 1)] for i in range(DTILES)]
                prev_v = ln_chunk(cs, prev_v, out_tiles=xnb_out)
                bank_wait(0)
                p.wait("tensor", "e_vector", prev_v)
                p.wait("tensor", "dq_moe0", 16)
                for k in range(DTILES):
                    p.op("tensor", "matmul", psb[0][0:16, 0:TC], gateW[:, 16 * k:16 * (k + 1)],
                         xnb_out[k], start=(k == 0), stop=(k == DTILES - 1))
                t_lg = p.tick("tensor")
                p.wait("scalar", "e_tensor", t_lg)
                p.op("scalar", "activation", out=sc_em[:, cs], in_=psb[0][0:16, 0:TC], func=AF.Sigmoid)
                s_sg = p.tick("scalar")
                bank_free[0] = ("e_scalar", s_sg)
                if DBG_STAGE == "msel":
                    p.wait("gpsimd", "e_scalar", s_sg)
                    p.dma("gpsimd", dbg_out[c][0:16, :], sc_em[:, cs], sem="dq_dbg")
                stok = arenaF[:, SEL:SEL + 64]
                sfc = arenaF[:, SEL + 64:SEL + 128]
                gl = arenaF[:, SEL + 128:SEL + 192]
                msk = arenaF[:, SEL + 192:SEL + 256]
                tmps = arenaF[:, SEL + 256:SEL + 320]
                accA = arenaF[:, SEL + 320:SEL + 384]
                colS = arenaF[:, SEL + 384:SEL + 400]
                p.wait("tensor", "e_scalar", s_sg)
                for g in range(4):
                    bank_wait(1 + g % 2)
                    p.op("tensor", "transpose", psb[1 + g % 2][:, 0:16],
                         sc_em[:, TC * c + 128 * g:TC * c + 128 * (g + 1)], ident[0:16, 0:16])
                    t_tr = p.tick("tensor")
                    p.wait("vector", "e_tensor", t_tr)
                    p.op("vector", "tensor_copy", out=stok[:, 16 * g:16 * (g + 1)], in_=psb[1 + g % 2][:, 0:16])
                    bank_free[1 + g % 2] = p.tick("vector")
                st3 = lambda ap: ap.rearrange("p (g e) -> p g e", e=16)
                p.op("vector", "tensor_add", out=st3(sfc), in0=st3(stok),
                     in1=ebias_b[:, None, :].broadcast_to([128, 4, 16]))
                g4 = lambda ap: ap.rearrange("p (g e) -> p g e", e=4)
                L_ = arenaF[:, SEL + 400:SEL + 416]
                l_ = arenaF[:, SEL + 416:SEL + 432]
                M_ = arenaF[:, SEL + 432:SEL + 448]
                m_ = arenaF[:, SEL + 448:SEL + 464]
                a16 = lambda ap: ap.rearrange("p (g e) -> p g e", e=1)
                p.op("vector", "tensor_max", out=a16(L_), in0=g4(sfc)[:, :, 0:1], in1=g4(sfc)[:, :, 1:2])
                p.op("vector", "tensor_tensor", out=a16(l_), in0=g4(sfc)[:, :, 0:1], in1=g4(sfc)[:, :, 1:2], op=ALU.min)
                p.op("vector", "tensor_max", out=a16(M_), in0=g4(sfc)[:, :, 2:3], in1=g4(sfc)[:, :, 3:4])
                p.op("vector", "tensor_tensor", out=a16(m_), in0=g4(sfc)[:, :, 2:3], in1=g4(sfc)[:, :, 3:4], op=ALU.min)
                t1_ = arenaF[:, SEL + 464:SEL + 480]
                t2_ = arenaF[:, SEL + 480:SEL + 496]
                p.op("vector", "tensor_max", out=t1_[:, 0:16], in0=L_[:, 0:16], in1=M_[:, 0:16])
                p.op("vector", "tensor_tensor", out=t2_[:, 0:16], in0=L_[:, 0:16], in1=M_[:, 0:16], op=ALU.min)
                p.op("vector", "tensor_max", out=l_[:, 0:16], in0=l_[:, 0:16], in1=m_[:, 0:16])
                p.op("vector", "tensor_max", out=t2_[:, 0:16], in0=t2_[:, 0:16], in1=l_[:, 0:16])
                p.op("vector", "tensor_add", out=gl[:, 0:16], in0=t1_[:, 0:16], in1=t2_[:, 0:16])
                glv = gl[:, 0:16].rearrange("p (tg eg) -> p tg eg", eg=4)
                rk = arenaF[:, SEL + 496:SEL + 512]
                rkv = rk.rearrange("p (tg eg) -> p tg eg", eg=4)
                a = arenaF[:, SEL + 512:SEL + 516]
                av = a.rearrange("p (tg e) -> p tg e", e=1)
                p.op("vector", "memset", rk[:, 0:16], 0.0)
                for ggg in range(4):
                    for j in range(4):
                        if j == ggg:
                            continue
                        p.op("vector", "tensor_tensor", out=av, in0=glv[:, :, j:j + 1],
                             in1=glv[:, :, ggg:ggg + 1], op=ALU.is_gt)
                        p.op("vector", "tensor_add", out=rkv[:, :, ggg:ggg + 1],
                             in0=rkv[:, :, ggg:ggg + 1], in1=av)
                        if j < ggg:
                            p.op("vector", "tensor_tensor", out=av, in0=glv[:, :, j:j + 1],
                                 in1=glv[:, :, ggg:ggg + 1], op=ALU.is_equal)
                            p.op("vector", "tensor_add", out=rkv[:, :, ggg:ggg + 1],
                                 in0=rkv[:, :, ggg:ggg + 1], in1=av)
                p.op("vector", "tensor_scalar", out=rk[:, 0:16], in0=rk[:, 0:16], scalar1=2.0,
                     scalar2=None, op0=ALU.is_lt)
                p.op("vector", "tensor_tensor", out=g4(tmps), in0=g4(sfc),
                     in1=rk[:, :, None].broadcast_to([128, 16, 4]), op=ALU.mult)
                p.op("vector", "memset", accA[:, 0:64], 0.0)
                mx = arenaF[:, SEL + 516:SEL + 520]
                ci = arenaF[:, SEL + 520:SEL + 584]
                for r in range(4):
                    p.op("vector", "tensor_reduce", out=mx[:, 0:4].rearrange("p (g e) -> p g e", e=1),
                         in_=st3(tmps), axis=AXX, op=ALU.max)
                    p.op("vector", "tensor_tensor", out=st3(ci), in0=st3(tmps),
                         in1=mx[:, 0:4].rearrange("p (g e) -> p g e", e=1).broadcast_to([128, 4, 16]), op=ALU.is_equal)
                    p.op("vector", "tensor_tensor", out=st3(ci), in0=st3(ci),
                         in1=iota_b[:, None, :].broadcast_to([128, 4, 16]), op=ALU.mult)
                    p.op("vector", "tensor_reduce", out=mx[:, 0:4].rearrange("p (g e) -> p g e", e=1),
                         in_=st3(ci), axis=AXX, op=ALU.max)
                    p.op("vector", "tensor_tensor", out=st3(ci), in0=st3(ci),
                         in1=mx[:, 0:4].rearrange("p (g e) -> p g e", e=1).broadcast_to([128, 4, 16]), op=ALU.is_equal)
                    p.op("vector", "tensor_add", out=st3(accA), in0=st3(accA), in1=st3(ci))
                    p.op("vector", "scalar_tensor_tensor", out=st3(tmps), in0=st3(ci),
                         scalar=-1e30, in1=st3(tmps), op0=ALU.mult, op1=ALU.add)
                p.op("vector", "tensor_mul", out=st3(accA), in0=st3(accA), in1=st3(stok))
                p.op("vector", "tensor_reduce", out=colS[:, 0:4].rearrange("p (g e) -> p g e", e=1),
                     in_=st3(accA), axis=AXX, op=ALU.add)
                p.op("vector", "tensor_scalar_add", out=colS[:, 0:4], in0=colS[:, 0:4], scalar1=1e-20)
                p.op("vector", "reciprocal", out=colS[:, 4:8], in_=colS[:, 0:4])
                p.op("vector", "tensor_scalar_mul", out=colS[:, 4:8], in0=colS[:, 4:8], scalar1=2.5)
                p.op("vector", "tensor_tensor", out=st3(accA), in0=st3(accA),
                     in1=colS[:, 4:8].rearrange("p (g e) -> p g e", e=1).broadcast_to([128, 4, 16]), op=ALU.mult)
                v_w = p.tick("vector")
                if DBG_STAGE == "msel" and c == 0:
                    p.wait("gpsimd", "e_vector", v_w)
                    p.dma("gpsimd", dbg_out[4][:, 0:64], stok, sem="dq_dbg")
                    p.dma("gpsimd", dbg_out[5][:, 0:64], sfc, sem="dq_dbg")
                    p.dma("gpsimd", dbg_out[6][:, 0:16], gl[:, 0:16], sem="dq_dbg")
                    p.dma("gpsimd", dbg_out[7][:, 0:16], rk[:, 0:16], sem="dq_dbg")
                    p.dma("gpsimd", dbg_out[8][:, 0:64], accA, sem="dq_dbg")
                    p.dma("gpsimd", dbg_out[9][:, 0:8], colS[:, 0:8], sem="dq_dbg")
                    p.wait("vector", "dq_dbg", p.cnt["dq_dbg"])
                p.wait("tensor", "e_vector", v_w)
                wst_val = 0
                for g in range(4):
                    bank_wait(1 + g % 2)
                    p.op("tensor", "transpose", psb[1 + g % 2][0:16, 0:128],
                         accA[:, 16 * g:16 * (g + 1)], ident)
                    t_tr2 = p.tick("tensor")
                    p.wait("scalar", "e_tensor", t_tr2)
                    if wst_val:
                        p.wait("scalar", "dq_wdram", wst_val)
                    p.op("scalar", "copy", out=wtmp_b, in_=psb[1 + g % 2][0:16, 0:128])
                    s_cp = p.tick("scalar")
                    bank_free[1 + g % 2] = ("e_scalar", s_cp)
                    p.wait("gpsimd", "e_scalar", s_cp)
                    wst_val, _ = p.dma("gpsimd", W_dram[:, TC * c + 128 * g:TC * c + 128 * (g + 1)],
                                       wtmp_b, sem="dq_wdram")

            y_banks = [(psb[4 + i // 2], (i % 2) * TCM) for i in range(8)]
            wl_guard = p.cnt["e_tensor"]
            prev_twr = 0
            prev_vin = 0
            p.wait("gpsimd", "dq_wdram", p.cnt["dq_wdram"])
            p.wait("vector", "dq_wdram", p.cnt["dq_wdram"])
            p.wait("sync", "dq_wdram", p.cnt["dq_wdram"])
            for e in range(moe_erange[0], moe_erange[1]):
                if wl_guard:
                    p.wait("gpsimd", "e_tensor", wl_guard)
                if e < 16:
                    lg, lgs = p.dma("gpsimd", wg_v.rearrange("p (k f) -> p k f", f=512),
                                    gpT_d[e].rearrange("(k p) f -> p k f", p=128), sem="dq_moew")
                    lu, lus = p.dma("gpsimd", wu_v.rearrange("p (k f) -> p k f", f=512),
                                    upT_d[e].rearrange("(k p) f -> p k f", p=128), sem="dq_moew")
                    ld_, lds = p.dma("gpsimd", wd_v.rearrange("p (k f) -> p k f", f=1024),
                                     dnT_d[e].rearrange("(k p) f -> p k f", p=128), sem="dq_moew")
                else:
                    lg, lgs = p.dma("gpsimd", wg_v.rearrange("p (k f) -> p k f", f=512),
                                    sgT_d.rearrange("(k p) f -> p k f", p=128), sem="dq_moew")
                    lu, lus = p.dma("gpsimd", wu_v.rearrange("p (k f) -> p k f", f=512),
                                    suT_d.rearrange("(k p) f -> p k f", p=128), sem="dq_moew")
                    ld_, lds = p.dma("gpsimd", wd_v.rearrange("p (k f) -> p k f", f=1024),
                                     sdT_d.rearrange("(k p) f -> p k f", p=128), sem="dq_moew")
                p.wait("tensor", "dq_moew", ld_)
                for c8 in range(4):
                    gcol = 1024 * half + TCM * c8
                    lcol = TCM * c8
                    xnb_c = [xnb_t[i][:, lcol:lcol + TCM] for i in range(DTILES)]
                    if e < 16:
                        if prev_twr:
                            p.wait("sync", "e_tensor", prev_twr)
                        wrl, wrls = p.dma("sync", wrow_b, W_dram[e:e + 1, gcol:gcol + TCM],
                                          sem="dq_wrow")
                        bank_wait(0)
                        p.wait("tensor", "dq_wrow", wrl)
                        p.op("tensor", "matmul", psb[0][:, 0:TCM], ones_rb, wrow_b,
                             start=True, stop=True)
                        t_wr = prev_twr = p.tick("tensor")
                        p.wait("scalar", "e_tensor", t_wr)
                        if prev_vin:
                            p.wait("scalar", "e_vector", prev_vin)
                        p.op("scalar", "copy", out=wrep_s, in_=psb[0][:, 0:TCM])
                        s_wr2 = p.tick("scalar")
                        bank_free[0] = ("e_scalar", s_wr2)
                    bank_wait(0, 1, 2, 3)
                    for j in range(4):
                        for k in range(DTILES):
                            p.op("tensor", "matmul", psb[j // 2][:, (j % 2) * TCM:(j % 2) * TCM + TCM],
                                 wg_v[:, 512 * k + 128 * j:512 * k + 128 * (j + 1)],
                                 xnb_c[k], start=(k == 0 and j % 2 == 0),
                                 stop=(k == DTILES - 1 and j % 2 == 1))
                        for k in range(DTILES):
                            p.op("tensor", "matmul", psb[2 + j // 2][:, (j % 2) * TCM:(j % 2) * TCM + TCM],
                                 wu_v[:, 512 * k + 128 * j:512 * k + 128 * (j + 1)],
                                 xnb_c[k], start=(k == 0 and j % 2 == 0),
                                 stop=(k == DTILES - 1 and j % 2 == 1))
                    t_gu = p.tick("tensor")
                    p.wait("scalar", "e_tensor", t_gu)
                    for j in range(4):
                        p.op("scalar", "activation", out=p_s[j // 2][:, (j % 2) * TCM:(j % 2) * TCM + TCM],
                             in_=psb[j // 2][:, (j % 2) * TCM:(j % 2) * TCM + TCM], func=AF.Silu)
                    s_si = p.tick("scalar")
                    p.wait("vector", "e_scalar", s_si)
                    if e < 16:
                        p.wait("vector", "e_scalar", s_wr2)
                    for j in range(4):
                        p.op("vector", "tensor_mul", out=inter_b[:, TCM * j:TCM * (j + 1)],
                             in0=p_s[j // 2][:, (j % 2) * TCM:(j % 2) * TCM + TCM],
                             in1=psb[2 + j // 2][:, (j % 2) * TCM:(j % 2) * TCM + TCM])
                    if e < 16:
                        for j in range(4):
                            p.op("vector", "tensor_mul", out=inter_b[:, TCM * j:TCM * (j + 1)],
                                 in0=inter_b[:, TCM * j:TCM * (j + 1)], in1=wrep_s)
                    v_in = prev_vin = p.tick("vector")
                    bank_free[0] = bank_free[1] = bank_free[2] = bank_free[3] = v_in
                    p.wait("tensor", "e_vector", v_in)
                    bank_wait(4, 5, 6, 7)
                    for j in range(8):
                        ybank, yoff = y_banks[j]
                        for k in range(4):
                            p.op("tensor", "matmul", ybank[:, yoff:yoff + TCM],
                                 wd_v[:, 1024 * k + 128 * j:1024 * k + 128 * (j + 1)],
                                 inter_b[:, TCM * k:TCM * (k + 1)],
                                 start=(k == 0 and j % 2 == 0), stop=(k == 3 and j % 2 == 1))
                    t_dn = p.tick("tensor")
                    p.wait("vector", "e_tensor", t_dn)
                    for j in range(8):
                        ybank, yoff = y_banks[j]
                        p.op("vector", "scalar_tensor_tensor", out=x_t[j][:, gcol:gcol + TCM],
                             in0=ybank[:, yoff:yoff + TCM], scalar=0.5,
                             in1=x_t[j][:, gcol:gcol + TCM], op0=ALU.mult, op1=ALU.add)
                    v_acc = p.tick("vector")
                    bank_free[4] = bank_free[5] = bank_free[6] = bank_free[7] = v_acc
                wl_guard = p.cnt["e_tensor"]
            prev_v = p.tick("vector")
        guards["wst2"] = p.cnt["e_tensor"]
        return prev_v

    SL = [arenaF[:, 2048 * i:2048 * (i + 1)] for i in range(8)]
    aB8 = [arenaB[:, 2048 * i:2048 * (i + 1)] for i in range(8)]
    hst = amax_s

    def emit_attn_sublayer(lyr, prev_v0):
        prev_v = prev_v0
        XP = 33 if lyr == 0 else 96
        Win_src = aWin_d if lyr == 0 else bWin_d
        Wout_src = aWout_d if lyr == 0 else bWout_d
        xpT_src = axpT_d if lyr == 0 else bxpT_d
        ar_in_d = ar0_i if lyr == 0 else ar1_i
        ar_out_d = ar0_o if lyr == 0 else ar1_o
        AOFF = 4 if lyr == 0 else 50
        DOFF = 36 if lyr == 0 else 82
        CWO = 38 if lyr == 0 else 84
        CBO = 46 if lyr == 0 else 92
        BROW = 1 if lyr == 0 else 64
        CROW = 17 if lyr == 0 else 80

        if lyr == 1:
            p.wait("gpsimd", "e_tensor", p.cnt["e_tensor"])
            p.wait("gpsimd", "e_vector", p.cnt["e_vector"])
            p.dma("gpsimd", inter_b[0:64, 0:128], minT_d, sem="dq_mst")
            p.dma("gpsimd", inter_b[0:64, 128:140], mxpT_d, sem="dq_mst")
            p.dma("gpsimd", inter_b[0:4, 140:204], mdtwT_d, sem="dq_mst")
            p.dma("gpsimd", inter_b[0:64, 204:268], moutT_d, sem="dq_mst")
            for t in range(2):
                p.dma("gpsimd", inter_b[0:64, 268 + 128 * t:268 + 128 * (t + 1)],
                      bdtwT_d[:, 128 * t:128 * (t + 1)], sem="dq_mst")
            p.wait("tensor", "dq_mst", p.cnt["dq_mst"])

        ln_vs = []
        ln_sc = 0
        for c in range(NCH):
            cs = slice(TC * c, TC * (c + 1))
            if lyr == 0:
                lv = ln_chunk(cs, prev_v, out_tiles=xn_t)
                bank_wait(6)
                p.wait("scalar", "e_vector", lv)
                t_red = stat_ssq(xn_t, None, ROW_A, DTILES)
                p.wait("vector", "e_tensor", t_red)
                p.op("vector", "tensor_scalar", out=r_var, in0=ROW_A,
                     scalar1=1.0 / D, scalar2=FEPS, op0=ALU.mult, op1=ALU.add)
                vv = p.tick("vector")
                bank_free[6] = vv
                p.wait("scalar", "e_vector", vv)
                p.op("scalar", "activation", out=r_var, in_=r_var, func=AF.Sqrt)
                sv = p.tick("scalar")
                p.wait("vector", "e_scalar", sv)
                p.op("vector", "reciprocal", out=r_rstd, in_=r_var)
                vv2 = p.tick("vector")
                p.wait("scalar", "e_vector", vv2)
                p.op("scalar", "copy", out=rr_a, in_=r_rstd)
                sv2 = p.tick("scalar")
                bank_wait(4)
                p.wait("tensor", "e_scalar", sv2)
                p.op("tensor", "matmul", psb[4], ones_rr, rr_a, start=True, stop=True)
                tb = p.tick("tensor")
                p.wait("vector", "e_tensor", tb)
                for i in range(DTILES):
                    p.op("vector", "tensor_mul", out=p_s[i % 2], in0=xn_t[i], in1=psb[4])
                    vt_ = p.tick("vector")
                    p.wait("scalar", "e_vector", vt_)
                    p.op("scalar", "activation", out=aB8[i][:, cs], in_=p_s[i % 2],
                         func=AF.Copy, scale=acols[:, 103 + i:104 + i])
                    p.tick("scalar")
                lv = p.tick("scalar")
                ln_sc = lv
                bank_free[4] = p.tick("vector")
                p.wait("vector", "e_scalar", lv)
                prev_v = p.tick("vector")
            else:
                prev_v = ln_chunk(cs, prev_v, out_tiles=[aB8[i][:, cs] for i in range(DTILES)])
            ln_vs.append(prev_v)

        tguard = p.cnt["e_tensor"]
        sguard = {}
        tmms = {}
        p.wait("tensor", "e_vector", prev_v)
        if lyr == 0:
            p.wait("tensor", "e_scalar", ln_sc)
        for po in range(32):
            if po >= 2:
                p.wait("gpsimd", "e_tensor", tmms[po - 2])
            else:
                p.wait("gpsimd", "e_tensor", tguard)
            lw, lws = p.dma("gpsimd", wst1[po % 2].rearrange("p (k m) -> p k m", m=128),
                            Win_src[:, 128 * po:128 * (po + 1)].rearrange("(k p) m -> p k m", p=128),
                            sem="dq_win")
            bs = (po % 2) * 4
            bank_wait(bs, bs + 1, bs + 2, bs + 3)
            p.wait("tensor", "dq_win", lw)
            for c4 in range(4):
                for k in range(DTILES):
                    p.op("tensor", "matmul", psb[bs + c4][:, 0:512],
                         wst1[po % 2][:, 128 * k:128 * (k + 1)],
                         aB8[k][:, 512 * c4:512 * (c4 + 1)],
                         start=(k == 0), stop=(k == DTILES - 1))
            tmms[po] = p.tick("tensor")
            p.wait("scalar", "e_tensor", tmms[po])
            for c4 in range(4):
                p.wait("scalar", "dq_sndx", sguard.get("sx" + str(c4 % 2), 0))
                p.op("scalar", "copy", out=p_s[c4 % 2], in_=psb[bs + c4][:, 0:512])
                sev = p.tick("scalar")
                bank_free[bs + c4] = ("e_scalar", sev)
                p.wait("sync", "e_scalar", sev)
                if po < 16:
                    dv, _ = p.dma("sync", snd_x[po // 2, po % 2][:, 512 * c4:512 * (c4 + 1)],
                                  p_s[c4 % 2], sem="dq_sndx")
                else:
                    dv, _ = p.dma("sync", z_sp[po - 16][:, 512 * c4:512 * (c4 + 1)],
                                  p_s[c4 % 2], sem="dq_sndx")
                sguard["sx" + str(c4 % 2)] = dv
        p.wait("gpsimd", "dq_sndx", p.cnt["dq_sndx"])
        c1 = p.coll("AllToAll", ALU.bypass, [snd_x.opt()], [rcv_x.opt()])

        p.wait("gpsimd", "e_tensor", tmms[30])
        p.wait("gpsimd", "e_tensor", tmms[31])
        for t in range(2):
            p.dma("gpsimd", wst1[0][:, XP * t:XP * (t + 1)], xpT_src[t], sem="dq_xp")
        p.wait("sync", "e_gpsimd", c1)
        p.wait("tensor", "dq_xp", p.cnt["dq_xp"])
        vprev = p.tick("vector")
        last_tx = 0
        for b in range(2):
            for tch in range(4):
                tsl = slice(2048 * tch, 2048 * (tch + 1))
                stg_ticks = {}
                for t in range(2):
                    lx, _ = p.dma("sync", SL[0], rcv_x[4 * b + tch, t], sem="dq_rx")
                    p.wait("vector", "dq_rx", lx)
                    p.wait("vector", "e_scalar", sguard.get("u", 0))
                    cw = lambda k: acols[:, CWO + 4 * t + k:CWO + 4 * t + k + 1]
                    p.op("vector", "tensor_scalar_mul", out=SL[1], in0=SL[0], scalar1=cw(3))
                    for sh in (1, 2, 3):
                        p.op("vector", "scalar_tensor_tensor", out=SL[1][:, sh:2048],
                             in0=SL[0][:, 0:2048 - sh], scalar=cw(3 - sh),
                             op0=ALU.mult, op1=ALU.add, in1=SL[1][:, sh:2048])
                    if tch > 0:
                        pt = hst[:, 64 + 4 * t:67 + 4 * t]
                        for j in range(3):
                            for i in range(j, 3):
                                p.op("vector", "scalar_tensor_tensor", out=SL[1][:, j:j + 1],
                                     in0=pt[:, i:i + 1], scalar=cw(j - i + 2),
                                     op0=ALU.mult, op1=ALU.add, in1=SL[1][:, j:j + 1])
                    p.op("vector", "tensor_copy", out=hst[:, 64 + 4 * t:67 + 4 * t],
                         in_=SL[0][:, 2045:2048])
                    vt = p.tick("vector")
                    p.wait("scalar", "e_vector", vt)
                    p.op("scalar", "activation", out=SL[0], in_=SL[1], func=AF.Silu,
                         bias=acols[:, CBO + t:CBO + t + 1])
                    if last_tx:
                        p.wait("scalar", "e_tensor", last_tx)
                    p.op("scalar", "copy", out=aB8[6 + t], in_=SL[0])
                    st = p.tick("scalar")
                    sguard["u"] = st
                    stg_ticks[t] = st
                    p.wait("sync", "e_scalar", st)
                    p.dma("sync", u_d[b, t][:, tsl], SL[0], sem="dq_ud")
                p.wait("tensor", "e_scalar", stg_ticks[1])
                bank_wait(0, 1, 2, 3)
                for c4 in range(4):
                    for t in range(2):
                        p.op("tensor", "matmul", psb[c4][0:XP, 0:512],
                             wst1[0][:, XP * t:XP * (t + 1)],
                             aB8[6 + t][:, 512 * c4:512 * (c4 + 1)],
                             start=(t == 0), stop=(t == 1))
                    tx = last_tx = p.tick("tensor")
                    p.wait("scalar", "e_tensor", tx)
                    p.wait("scalar", "dq_ari", sguard.get("ari", 0))
                    p.op("scalar", "copy", out=p_s[c4 % 2][0:XP, :], in_=psb[c4][0:XP, 0:512])
                    sc = p.tick("scalar")
                    bank_free[c4] = ("e_scalar", sc)
                    p.wait("sync", "e_scalar", sc)
                    av, _ = p.dma("sync", ar_in_d[b][0:XP, 2048 * tch + 512 * c4:
                                                    2048 * tch + 512 * (c4 + 1)],
                                  p_s[c4 % 2][0:XP, :], sem="dq_ari")
                    if c4 >= 1:
                        sguard["ari"] = p.cnt["dq_ari"] - 16
        p.wait("gpsimd", "dq_ari", p.cnt["dq_ari"])
        c2 = p.coll("AllReduce", ALU.add, [ar_in_d.opt()], [ar_out_d.opt()])
        p.wait("sync", "e_gpsimd", c2)
        p.wait("gpsimd", "e_gpsimd", c2)

        if lyr == 1:
            m_in = inter_b[0:64, 0:128]
            m_xp = inter_b[0:64, 128:140]
            m_dtw = inter_b[0:4, 140:204]
            m_out = inter_b[0:64, 204:268]
            mA = lambda n: acols[0:64, 94 + n:95 + n]
            mD = acols[0:64, 98:99]
            mnb = acols[0:64, 99:100]
            mcw = lambda k: acols[0:64, 100 + k:101 + k]
            mcb = acols[0:64, 102:103]
            vp = p.tick("vector")
            for b in range(2):
                for tch in range(4):
                    tsl = slice(2048 * tch, 2048 * (tch + 1))
                    l0, _ = p.dma("sync", SL[0][0:64, :], ar_out_d[b][0:64, tsl], sem="dq_rx")
                    p.wait("scalar", "dq_rx", l0)
                    p.wait("scalar", "e_vector", vp)
                    p.op("scalar", "copy", out=aB8[6][0:64, :], in_=SL[0][0:64, :])
                    s0 = p.tick("scalar")
                    bank_wait(0, 1, 2, 3)
                    p.wait("tensor", "e_scalar", s0)
                    for c4 in range(4):
                        p.op("tensor", "matmul", psb[c4][:, 0:512], m_in,
                             aB8[6][0:64, 512 * c4:512 * (c4 + 1)], start=True, stop=True)
                    t0 = p.tick("tensor")
                    p.wait("scalar", "e_tensor", t0)
                    zv = 0
                    for c4 in range(4):
                        p.wait("scalar", "dq_zm", sguard.get("zm" + str(c4 % 2), 0))
                        p.op("scalar", "copy", out=p_s[c4 % 2], in_=psb[c4][:, 0:512])
                        p.op("scalar", "copy", out=SL[0][0:64, 512 * c4:512 * (c4 + 1)],
                             in_=p_s[c4 % 2][0:64, :])
                        s1c = p.tick("scalar")
                        bank_free[c4] = ("e_scalar", s1c)
                        p.wait("sync", "e_scalar", s1c)
                        zv, _ = p.dma("sync", SL[1][0:64, 512 * c4:512 * (c4 + 1)],
                                      p_s[c4 % 2][64:128, :], sem="dq_zm")
                        sguard["zm" + str(c4 % 2)] = zv
                    s1 = p.tick("scalar")
                    p.wait("vector", "e_scalar", s1)
                    p.op("vector", "tensor_scalar_mul", out=SL[2][0:64, :], in0=SL[0][0:64, :],
                         scalar1=mcw(1))
                    p.op("vector", "scalar_tensor_tensor", out=SL[2][0:64, 1:2048],
                         in0=SL[0][0:64, 0:2047], scalar=mcw(0), op0=ALU.mult, op1=ALU.add,
                         in1=SL[2][0:64, 1:2048])
                    if tch > 0:
                        p.op("vector", "scalar_tensor_tensor", out=SL[2][0:64, 0:1],
                             in0=hst[0:64, 72 + b:73 + b], scalar=mcw(0), op0=ALU.mult,
                             op1=ALU.add, in1=SL[2][0:64, 0:1])
                    p.op("vector", "tensor_copy", out=hst[0:64, 72 + b:73 + b],
                         in_=SL[0][0:64, 2047:2048])
                    v1 = p.tick("vector")
                    p.wait("scalar", "e_vector", v1)
                    p.op("scalar", "activation", out=SL[0][0:64, :], in_=SL[2][0:64, :],
                         func=AF.Silu, bias=mcb)
                    p.op("scalar", "copy", out=aB8[6][0:64, :], in_=SL[0][0:64, :])
                    s2 = p.tick("scalar")
                    bank_wait(0, 1, 2, 3)
                    p.wait("tensor", "e_scalar", s2)
                    for c4 in range(4):
                        p.op("tensor", "matmul", psb[c4][0:12, 0:512], m_xp,
                             aB8[6][0:64, 512 * c4:512 * (c4 + 1)], start=True, stop=True)
                    t1 = p.tick("tensor")
                    p.wait("scalar", "e_tensor", t1)
                    p.wait("scalar", "dq_xdm", sguard.get("xdm", 0))
                    for c4 in range(4):
                        p.op("scalar", "copy", out=p_s[0][0:12, 0:512], in_=psb[c4][0:12, 0:512])
                        p.op("scalar", "copy", out=aB8[7][0:4, 512 * c4:512 * (c4 + 1)],
                             in_=psb[c4][0:4, 0:512])
                        s3 = p.tick("scalar")
                        p.wait("sync", "e_scalar", s3)
                        xv, _ = p.dma("sync", xdm_d[b][:, 2048 * tch + 512 * c4:
                                                       2048 * tch + 512 * (c4 + 1)],
                                      p_s[0][0:12, 0:512], sem="dq_xdm")
                        p.wait("scalar", "dq_xdm", xv)
                    sguard["xdm"] = p.cnt["dq_xdm"]
                    s3b = p.tick("scalar")
                    bank_free[0] = bank_free[1] = bank_free[2] = bank_free[3] = ("e_scalar", s3b)
                    bank_wait(4, 5, 6, 7)
                    p.wait("tensor", "e_scalar", s3b)
                    for c4 in range(4):
                        p.op("tensor", "matmul", psb[4 + c4][0:64, 0:512], m_dtw,
                             aB8[7][0:4, 512 * c4:512 * (c4 + 1)], start=True, stop=True)
                    t2 = p.tick("tensor")
                    p.wait("scalar", "e_tensor", t2)
                    for c4 in range(4):
                        p.op("scalar", "activation", out=SL[2][0:64, 512 * c4:512 * (c4 + 1)],
                             in_=psb[4 + c4][0:64, 0:512], func=AF.Sigmoid, scale=-1.0, bias=mnb)
                    p.op("scalar", "activation", out=SL[2][0:64, :], in_=SL[2][0:64, :], func=AF.Ln)
                    s4 = p.tick("scalar")
                    bank_free[4] = bank_free[5] = bank_free[6] = bank_free[7] = ("e_scalar", s4)
                    p.wait("vector", "e_scalar", s4)
                    p.wait("vector", "dq_zm", zv)
                    p.op("vector", "tensor_mul", out=SL[4][0:64, :], in0=SL[0][0:64, :],
                         in1=SL[2][0:64, :])
                    p.op("vector", "memset", SL[5][0:64, :], 0.0)
                    if tch == 0:
                        p.op("vector", "memset", hst[0:64, 76:80], 0.0)
                    vpre = p.tick("vector")
                    p.wait("gpsimd", "dq_xdm", p.cnt["dq_xdm"])
                    for n in range(4):
                        p.wait("gpsimd", "e_vector", vpre)
                        bb, _ = p.dma("gpsimd", aB8[n % 2][0:64, :],
                                      xdm_d[b][4 + n:5 + n, tsl].broadcast_to([64, 2048]),
                                      sem="dq_bc")
                        cb_, _ = p.dma("gpsimd", aB8[2 + n % 2][0:64, :],
                                       xdm_d[b][8 + n:9 + n, tsl].broadcast_to([64, 2048]),
                                       sem="dq_bc")
                        p.wait("scalar", "e_vector", vpre)
                        p.op("scalar", "activation", out=SL[3][0:64, :], in_=SL[2][0:64, :],
                             func=AF.Exp, scale=mA(n))
                        sa = p.tick("scalar")
                        p.wait("vector", "dq_bc", cb_)
                        p.wait("vector", "e_scalar", sa)
                        p.op("vector", "tensor_mul", out=SL[6][0:64, :], in0=SL[4][0:64, :],
                             in1=aB8[n % 2][0:64, :])
                        p.op("vector", "tensor_tensor_scan", SL[7][0:64, :], SL[3][0:64, :],
                             SL[6][0:64, :], hst[0:64, 76 + n:77 + n], ALU.mult, ALU.add)
                        p.op("vector", "tensor_copy", out=hst[0:64, 76 + n:77 + n],
                             in_=SL[7][0:64, 2047:2048])
                        p.op("vector", "tensor_mul", out=SL[3][0:64, :], in0=SL[7][0:64, :],
                             in1=aB8[2 + n % 2][0:64, :])
                        p.op("vector", "tensor_sub", out=SL[5][0:64, :], in0=SL[5][0:64, :],
                             in1=SL[3][0:64, :])
                        vpre = p.tick("vector")
                    p.op("vector", "scalar_tensor_tensor", out=SL[5][0:64, :], in0=SL[0][0:64, :],
                         scalar=mD, op0=ALU.mult, op1=ALU.add, in1=SL[5][0:64, :])
                    v2 = p.tick("vector")
                    p.wait("scalar", "e_vector", v2)
                    p.op("scalar", "activation", out=SL[6][0:64, :], in_=SL[1][0:64, :], func=AF.Silu)
                    s5 = p.tick("scalar")
                    p.wait("vector", "e_scalar", s5)
                    p.op("vector", "tensor_mul", out=SL[5][0:64, :], in0=SL[5][0:64, :],
                         in1=SL[6][0:64, :])
                    v3 = p.tick("vector")
                    p.wait("scalar", "e_vector", v3)
                    p.op("scalar", "copy", out=aB8[6][0:64, :], in_=SL[5][0:64, :])
                    s6 = p.tick("scalar")
                    bank_wait(0, 1, 2, 3)
                    p.wait("tensor", "e_scalar", s6)
                    for c4 in range(4):
                        p.op("tensor", "matmul", psb[c4][0:64, 0:512], m_out,
                             aB8[6][0:64, 512 * c4:512 * (c4 + 1)], start=True, stop=True)
                    t3 = p.tick("tensor")
                    p.wait("scalar", "e_tensor", t3)
                    p.wait("scalar", "dq_dtm", sguard.get("dtm", 0))
                    for c4 in range(4):
                        p.op("scalar", "copy", out=p_s[1][0:64, 0:512], in_=psb[c4][0:64, 0:512])
                        s7 = p.tick("scalar")
                        p.wait("sync", "e_scalar", s7)
                        dv, _ = p.dma("sync", dtm_d[b][:, 2048 * tch + 512 * c4:
                                                       2048 * tch + 512 * (c4 + 1)],
                                      p_s[1][0:64, 0:512], sem="dq_dtm")
                        p.wait("scalar", "dq_dtm", dv)
                    sguard["dtm"] = p.cnt["dq_dtm"]
                    s8 = p.tick("scalar")
                    bank_free[0] = bank_free[1] = bank_free[2] = bank_free[3] = ("e_scalar", s8)
                    vp = p.tick("vector")

        if lyr == 1:
            dstat = [inter_b[0:64, 268 + 128 * t:268 + 128 * (t + 1)] for t in range(2)]
        vprev = p.tick("vector")
        gprev = p.cnt["e_gpsimd"]
        for b in range(2):
            for tch in range(4):
                tsl = slice(2048 * tch, 2048 * (tch + 1))
                if lyr == 0:
                    l0, _ = p.dma("sync", SL[6], ar_out_d[b][0:1, tsl].broadcast_to([128, 2048]),
                                  sem="dq_rx")
                    p.wait("scalar", "dq_rx", l0)
                    p.wait("scalar", "e_vector", vprev)
                    for t in range(2):
                        p.op("scalar", "activation", out=SL[7], in_=SL[6], func=AF.Sigmoid,
                             scale=acols[:, 0 + t:1 + t], bias=acols[:, 2 + t:3 + t])
                        p.op("scalar", "activation", out=aB8[2 + t], in_=SL[7], func=AF.Ln)
                    sdt = p.tick("scalar")
                else:
                    p.wait("gpsimd", "dq_dtm", p.cnt["dq_dtm"])
                    ldm, _ = p.dma("gpsimd", aB8[6][0:64, :], dtm_d[b][0:64, tsl], sem="dq_bc")
                    p.wait("tensor", "dq_bc", ldm)
                    p.wait("tensor", "e_vector", vprev)
                    for t in range(2):
                        bs = 4 * t
                        bank_wait(bs, bs + 1, bs + 2, bs + 3)
                        for c4 in range(4):
                            p.op("tensor", "matmul", psb[bs + c4][:, 0:512], dstat[t],
                                 aB8[6][0:64, 512 * c4:512 * (c4 + 1)], start=True, stop=True)
                        tdl = p.tick("tensor")
                        p.wait("scalar", "e_tensor", tdl)
                        p.wait("scalar", "e_vector", vprev)
                        for c4 in range(4):
                            p.op("scalar", "activation", out=SL[7][:, 512 * c4:512 * (c4 + 1)],
                                 in_=psb[bs + c4][:, 0:512], func=AF.Sigmoid, scale=-1.0,
                                 bias=acols[:, 48 + t:49 + t])
                        p.op("scalar", "activation", out=aB8[2 + t], in_=SL[7], func=AF.Ln)
                        sdt = p.tick("scalar")
                        bank_free[bs] = bank_free[bs + 1] = bank_free[bs + 2] = \
                            bank_free[bs + 3] = ("e_scalar", sdt)
                if DBG_STAGE == "bscan" and b == 0 and tch == 0:
                    p.wait("gpsimd", "e_scalar", sdt)
                    for j in range(4):
                        p.dma("gpsimd", dbg_out[0 + j], aB8[2][:, 512 * j:512 * (j + 1)], sem="dq_dbg")
                        p.dma("gpsimd", dbg_out[4 + j], aB8[3][:, 512 * j:512 * (j + 1)], sem="dq_dbg")
                lu0, _ = p.dma("sync", SL[0], u_d[b, 0][:, tsl], sem="dq_rx")
                lu1, _ = p.dma("sync", SL[1], u_d[b, 1][:, tsl], sem="dq_rx")
                p.wait("vector", "dq_rx", lu1)
                p.wait("vector", "e_scalar", sdt)
                if p.cnt.get("dq_sndy", 0):
                    p.wait("vector", "dq_sndy", p.cnt["dq_sndy"])
                p.op("vector", "tensor_mul", out=SL[2], in0=SL[0], in1=aB8[2])
                p.op("vector", "tensor_mul", out=SL[3], in0=SL[1], in1=aB8[3])
                p.op("vector", "memset", SL[4], 0.0)
                p.op("vector", "memset", SL[5], 0.0)
                if tch == 0:
                    p.op("vector", "memset", hst[:, 0:32], 0.0)
                vpre = p.tick("vector")
                if DBG_STAGE == "bscan" and b == 0 and tch == 0:
                    p.wait("gpsimd", "e_vector", vpre)
                    for j in range(4):
                        p.dma("gpsimd", dbg_out[8 + j], SL[2][:, 512 * j:512 * (j + 1)], sem="dq_dbg")
                        p.dma("gpsimd", dbg_out[12 + j], SL[3][:, 512 * j:512 * (j + 1)], sem="dq_dbg")
                    p.wait("vector", "dq_dbg", p.cnt["dq_dbg"])
                gtick = 0
                for n in range(16):
                    bb, _ = p.dma("gpsimd", aB8[n % 2],
                                  ar_out_d[b][BROW + n:BROW + n + 1, tsl].broadcast_to([128, 2048]),
                                  sem="dq_bc")
                    cb_, _ = p.dma("gpsimd", aB8[4 + n % 2],
                                   ar_out_d[b][CROW + n:CROW + n + 1, tsl].broadcast_to([128, 2048]),
                                   sem="dq_bc")
                    for t in range(2):
                        p.wait("scalar", "e_vector", vpre)
                        if gtick:
                            p.wait("scalar", "e_gpsimd", gtick)
                        p.op("scalar", "activation", out=SL[6], in_=aB8[2 + t], func=AF.Exp,
                             scale=acols[:, AOFF + 16 * t + n:AOFF + 16 * t + n + 1])
                        sa = p.tick("scalar")
                        p.wait("vector", "dq_bc", cb_)
                        p.wait("vector", "e_scalar", sa)
                        if gtick:
                            p.wait("vector", "e_gpsimd", gtick)
                        p.op("vector", "tensor_mul", out=SL[0], in0=SL[2 + t], in1=aB8[n % 2])
                        p.op("vector", "tensor_tensor_scan", SL[1], SL[6], SL[0],
                             hst[:, 16 * t + n:16 * t + n + 1], ALU.mult, ALU.add)
                        p.op("vector", "tensor_copy", out=hst[:, 16 * t + n:16 * t + n + 1],
                             in_=SL[1][:, 2047:2048])
                        p.op("vector", "tensor_mul", out=SL[6], in0=SL[1], in1=aB8[4 + n % 2])
                        vpre = p.tick("vector")
                        if DBG_STAGE == "bscan" and b == 0 and tch == 0 and n == 0 and t == 0:
                            p.wait("gpsimd", "e_vector", vpre)
                            for j in range(4):
                                p.dma("gpsimd", dbg_out[16 + j], SL[0][:, 512 * j:512 * (j + 1)], sem="dq_dbg")
                                p.dma("gpsimd", dbg_out[20 + j], SL[1][:, 512 * j:512 * (j + 1)], sem="dq_dbg")
                                p.dma("gpsimd", dbg_out[24 + j], aB8[0][:, 512 * j:512 * (j + 1)], sem="dq_dbg")
                                p.dma("gpsimd", dbg_out[28 + j], aB8[4][:, 512 * j:512 * (j + 1)], sem="dq_dbg")
                            p.wait("vector", "dq_dbg", p.cnt["dq_dbg"])
                        p.wait("gpsimd", "e_vector", vpre)
                        p.op("gpsimd", "tensor_sub", out=SL[4 + t], in0=SL[4 + t], in1=SL[6])
                        gtick = p.tick("gpsimd")
                p.wait("vector", "e_gpsimd", gtick)
                p.wait("sync", "e_gpsimd", gtick)
                ys = {}
                for t in range(2):
                    lu, _ = p.dma("sync", SL[6 + t], u_d[b, t][:, tsl], sem="dq_rx")
                    p.wait("vector", "dq_rx", lu)
                    p.op("vector", "scalar_tensor_tensor", out=SL[4 + t], in0=SL[6 + t],
                         scalar=acols[:, DOFF + t:DOFF + t + 1], op0=ALU.mult, op1=ALU.add,
                         in1=SL[4 + t])
                    vy = p.tick("vector")
                    p.wait("sync", "e_vector", vy)
                    yv, _ = p.dma("sync", snd_y[4 * b + tch, t], SL[4 + t], sem="dq_sndy")
                vprev = p.tick("vector")
        p.wait("gpsimd", "dq_sndy", p.cnt["dq_sndy"])
        c3 = p.coll("AllToAll", ALU.bypass, [snd_y.opt()], [rcv_y.opt()])
        p.wait("sync", "e_gpsimd", c3)

        tguard2 = p.cnt["e_tensor"]
        wo_guard = {}
        prev_vt = p.tick("vector")
        for tc in range(NCH):
            cs = slice(TC * tc, TC * (tc + 1))
            if lyr == 0:
                lnv = ln_chunk(cs, prev_vt, out_tiles=xn_t)
                prev_vt = lnv
            tkm = {}
            stg_v = 0
            for k in range(16):
                if k >= 2:
                    p.wait("gpsimd", "e_tensor", tkm[k - 2])
                else:
                    p.wait("gpsimd", "e_tensor", wo_guard.get(k % 2, tguard2))
                lw, _ = p.dma("gpsimd", wst2[0][:, 1024 * (k % 2):1024 * (k % 2 + 1)]
                              .rearrange("p (a m) -> p a m", m=128),
                              Wout_src[128 * k:128 * (k + 1), :].rearrange("p (a m) -> p a m", m=128),
                              sem="dq_wo")
                if stg_v:
                    p.wait("sync", "e_vector", stg_v)
                ly, _ = p.dma("sync", h_t[8], rcv_y[k // 2, k % 2][:, cs], sem="dq_rx")
                lz, _ = p.dma("sync", h_t[9], z_sp[k][:, cs], sem="dq_rx")
                p.wait("scalar", "dq_rx", lz)
                p.wait("scalar", "e_vector", stg_v)
                p.op("scalar", "activation", out=h_t[10], in_=h_t[9], func=AF.Silu)
                sz = p.tick("scalar")
                p.wait("vector", "e_scalar", sz)
                if k >= 2:
                    p.wait("vector", "e_tensor", tkm[k - 2])
                p.op("vector", "tensor_mul", out=q2_t[16 + (k % 2)], in0=h_t[8], in1=h_t[10])
                stg_v = p.tick("vector")
                if k == 0:
                    bank_wait(0, 1, 2, 3, 4, 5, 6, 7)
                p.wait("tensor", "dq_wo", lw)
                p.wait("tensor", "e_vector", stg_v)
                for po in range(8):
                    p.op("tensor", "matmul", psb[po][:, 0:512],
                         wst2[0][:, 1024 * (k % 2) + 128 * po:1024 * (k % 2) + 128 * (po + 1)],
                         q2_t[16 + (k % 2)], start=(k == 0), stop=(k == 15))
                tkm[k] = p.tick("tensor")
                wo_guard[k % 2] = tkm[k]
            p.wait("vector", "e_tensor", tkm[15])
            for po in range(8):
                if lyr == 0:
                    p.op("vector", "tensor_add", out=p_s[po % 2], in0=psb[po], in1=xn_t[po])
                    p.op("vector", "tensor_add", out=x_t[po][:, cs], in0=x_t[po][:, cs],
                         in1=p_s[po % 2])
                else:
                    p.op("vector", "scalar_tensor_tensor", out=x_t[po][:, cs], in0=psb[po],
                         scalar=1.0, op0=ALU.mult, op1=ALU.add, in1=x_t[po][:, cs])
            prev_vt = p.tick("vector")
            for po in range(8):
                bank_free[po] = prev_vt
        guards["wst2"] = p.cnt["e_tensor"]
        guards["wst1"] = p.cnt["e_tensor"]
        return prev_vt

    p.wait("scalar", "dq_boot", BOOT)
    p.wait("tensor", "dq_boot", BOOT)
    p.wait("vector", "dq_q1ws", wq1_done)
    p.wait("vector", "dq_q2ws", wq2_done)
    PROLOG_V = p.cnt["e_vector"]
    if subs is None:
        subs = ["ffn0", "attn0", "ffn0", "moe1", "attn1", "moe1"]
    pv = PROLOG_V
    for s_name in subs:
        if s_name == "ffn0":
            pv = emit_ffn_sublayer(4, 6, pv)
        elif s_name == "moe1":
            pv = emit_moe_sublayer(pv)
        elif s_name == "attn0":
            pv = emit_attn_sublayer(0, pv)
        elif s_name == "attn1":
            pv = emit_attn_sublayer(1, pv)

    vv = p.tick("vector")
    ss = p.tick("scalar")
    tt = p.tick("tensor")
    p.wait("sync", "e_vector", vv)
    p.wait("sync", "e_scalar", ss)
    p.wait("sync", "e_tensor", tt)
    for i in range(DTILES):
        p.dma("sync", x_out[128 * i:128 * (i + 1), :], x_t[i], sem="dq_out")
    p.wait("gpsimd", "dq_out", p.cnt["dq_out"])

    p.finalize()
    return nc


def prep_in_maps(inputs):
    import ml_dtypes
    x = np.asarray(inputs["L0_attn_x"])
    gate_wT = np.ascontiguousarray(np.asarray(inputs["L1_ffn_gate_w"]).T)
    ebias = np.asarray(inputs["L1_ffn_e_bias"]).reshape(1, 16)
    gpT = np.ascontiguousarray(np.asarray(inputs["L1_ffn_gate_proj"]).transpose(0, 2, 1))
    upT = np.ascontiguousarray(np.asarray(inputs["L1_ffn_up_proj"]).transpose(0, 2, 1))
    dnT = np.ascontiguousarray(np.asarray(inputs["L1_ffn_down_proj"]).transpose(0, 2, 1))
    sgT = np.ascontiguousarray(np.asarray(inputs["L1_ffn_shared_gate_w"]).T)
    suT = np.ascontiguousarray(np.asarray(inputs["L1_ffn_shared_up_w"]).T)
    sdT = np.ascontiguousarray(np.asarray(inputs["L1_ffn_shared_down_w"]).T)
    iota_desc = np.broadcast_to(np.arange(16, 0, -1, dtype=np.float32)[None, :], (128, 16)).copy()
    ones_b = np.ones((1, 128), ml_dtypes.bfloat16)
    w1T = np.ascontiguousarray(np.asarray(inputs["L0_ffn_w1"]).T)
    w2T = np.ascontiguousarray(np.asarray(inputs["L0_ffn_w2"]).T)
    b1c = np.ascontiguousarray(np.asarray(inputs["L0_ffn_b1"]).reshape(32, 128).T)
    b2c = np.ascontiguousarray(np.asarray(inputs["L0_ffn_b2"]).reshape(8, 128).T)
    seed = np.full((1, 1), 2.0 ** -40, np.float32)
    ident = np.eye(128, dtype=np.float32)
    f32 = np.float32
    aW_in = np.ascontiguousarray(np.asarray(inputs["L0_attn_W_in"], f32))
    aW_out = np.ascontiguousarray(np.asarray(inputs["L0_attn_W_out"], f32))
    bW_inT = np.ascontiguousarray(np.asarray(inputs["L1_attn_in_proj_w"], f32).T)
    bW_outT = np.ascontiguousarray(np.asarray(inputs["L1_attn_out_proj_w"], f32).T)
    m_inT = np.ascontiguousarray(np.asarray(inputs["L1_attn_m_in_w"], f32).T)
    m_xpT = np.ascontiguousarray(np.asarray(inputs["L1_attn_m_x_proj_w"], f32).T)
    m_dtwT = np.ascontiguousarray(np.asarray(inputs["L1_attn_m_dt_w"], f32).T)
    m_outT = np.ascontiguousarray(np.asarray(inputs["L1_attn_m_out_w"], f32).T)
    a_cw = np.asarray(inputs["L0_attn_conv_w"], f32).reshape(2048, 4)
    b_cw = np.asarray(inputs["L1_attn_conv_w"], f32)
    a_wxp = np.asarray(inputs["L0_attn_W_xproj"], f32)
    b_wxpT = np.asarray(inputs["L1_attn_x_proj_w"], f32).T
    b_dtw = np.asarray(inputs["L1_attn_dt_out_w"], f32)
    nw_t = np.ascontiguousarray(np.asarray(inputs["L0_attn_norm_w"], f32).reshape(8, 128).T)

    def tilecol(v, ds):
        return v[ds].reshape(2, 128).T

    maps = []
    for c in range(NCORES):
        b, q = c // 4, c % 4
        xin = np.ascontiguousarray(x[b, TL * q:TL * (q + 1), :].T)
        ds = slice(256 * c, 256 * (c + 1))
        ac = np.zeros((128, 112), f32)
        ac[:, 0:2] = tilecol(np.asarray(inputs["L0_attn_W_dt"], f32).reshape(2048), ds)
        ac[:, 2:4] = tilecol(np.asarray(inputs["L0_attn_b_dt"], f32), ds)
        ac[:, 4:36] = np.asarray(inputs["L0_attn_A_log"], f32)[ds].reshape(2, 128, 16)\
            .transpose(1, 0, 2).reshape(128, 32)
        ac[:, 36:38] = tilecol(np.asarray(inputs["L0_attn_D"], f32), ds)
        ac[:, 38:46] = a_cw[ds].reshape(2, 128, 4).transpose(1, 0, 2).reshape(128, 8)
        ac[:, 46:48] = tilecol(np.asarray(inputs["L0_attn_conv_b"], f32), ds)
        ac[:, 48:50] = tilecol(np.asarray(inputs["L1_attn_dt_out_b"], f32), ds)
        ac[:, 50:82] = np.asarray(inputs["L1_attn_A_log"], f32)[ds].reshape(2, 128, 16)\
            .transpose(1, 0, 2).reshape(128, 32)
        ac[:, 82:84] = tilecol(np.asarray(inputs["L1_attn_D"], f32), ds)
        ac[:, 84:92] = b_cw[ds].reshape(2, 128, 4).transpose(1, 0, 2).reshape(128, 8)
        ac[:, 92:94] = tilecol(np.asarray(inputs["L1_attn_conv_b"], f32), ds)
        ac[0:64, 94:98] = np.asarray(inputs["L1_attn_m_A_log"], f32)
        ac[0:64, 98] = np.asarray(inputs["L1_attn_m_D"], f32)
        ac[0:64, 99] = np.asarray(inputs["L1_attn_m_dt_b"], f32)
        ac[0:64, 100:102] = np.asarray(inputs["L1_attn_m_conv_w"], f32)
        ac[0:64, 102] = np.asarray(inputs["L1_attn_m_conv_b"], f32)
        ac[:, 103:111] = nw_t
        a_xpT = np.ascontiguousarray(a_wxp[ds].reshape(2, 128, 33))
        b_xpT = np.ascontiguousarray(b_wxpT[ds].reshape(2, 128, 96))
        b_dtwT = np.ascontiguousarray(b_dtw[ds].T)
        maps.append(dict(x_in=xin, L0_w1T=w1T, L0_b1c=b1c, L0_w2T=w2T, L0_b2c=b2c,
                         seed=seed, ident=ident, L1_gate_wT=gate_wT, L1_ebias=ebias,
                         L1_gpT=gpT, L1_upT=upT, L1_dnT=dnT, L1_sgT=sgT, L1_suT=suT,
                         L1_sdT=sdT, iota_desc=iota_desc, ones_b=ones_b,
                         acols_in=np.ascontiguousarray(ac), aW_in=aW_in, aW_out=aW_out,
                         bW_inT=bW_inT, bW_outT=bW_outT, a_xpT=a_xpT, b_xpT=b_xpT,
                         b_dtwT=b_dtwT, m_inT=m_inT, m_xpT=m_xpT, m_dtwT=m_dtwT,
                         m_outT=m_outT))
    return maps


def kernel(**inputs):
    nc = build_program(subs=SUBS if 'SUBS' in globals() and SUBS else None)
    maps = prep_in_maps(inputs)
    res = run_bass_kernel_spmd(nc, maps, core_ids=list(range(NCORES)))
    out = np.empty((B, S, D), np.float32)
    for c in range(NCORES):
        b, q = c // 4, c % 4
        out[b, TL * q:TL * (q + 1), :] = res.results[c]["x_out"].T
    return out
